# version 03
# baseline (speedup 1.0000x reference)
import numpy as np
import ml_dtypes
import concourse.bass as bass
import concourse.mybir as mybir
from concourse.bass_utils import run_bass_kernel_spmd
from contextlib import ExitStack

F32 = mybir.dt.float32
F32R = mybir.dt.float32r
BF16 = mybir.dt.bfloat16
AF = mybir.ActivationFunctionType
ALU = mybir.AluOpType

D = 2048
B = 4
S = 2048
NCORES = 8
TOK = 1024
KC = D // 128
LN_EPS = 1e-5

H0, HD0 = 32, 64
H1, HD1 = 16, 128
Q_RANK, KV_RANK, ROPE = 1536, 512, 64
I0 = 8192
NG = 16



class Ctr:
    def __init__(self, handle):
        self.h = handle
        self.v = 0

    def add(self, k):
        self.v += k
        return self.v


class _Op:
    __slots__ = ("name", "args", "kwargs", "incs", "self_wait")

    def __init__(self, name, args, kwargs):
        self.name = name
        self.args = args
        self.kwargs = kwargs
        self.incs = []
        self.self_wait = None

    def inc(self, ctr, k=1):
        ctr.add(k)
        self.incs.append((ctr, k))
        return self


class Eng:

    def __init__(self, q, auto_ctr=None):
        self._q = q
        self._auto = auto_ctr

    def wait(self, ctr, v=None):
        self._q.append(_Op("__wait__", (ctr, ctr.v if v is None else v), {}))

    def __getattr__(self, nm):
        def rec(*a, **k):
            op = _Op(nm, a, k)
            if self._auto is not None and nm != "dma_start":
                op.self_wait = (self._auto, self._auto.v)
                op.incs.append((self._auto, 1))
                self._auto.add(1)
            self._q.append(op)
            return op
        return rec


class Prog:
    ENGS = ("sync", "scalar", "vector", "tensor", "gpsimd")

    def __init__(self, nc, ctx):
        self.nc = nc
        self.ctx = ctx
        self.q = {e: [] for e in self.ENGS}

    def engs(self, s_ac=None, s_ve=None):
        return (Eng(self.q["sync"]), Eng(self.q["gpsimd"]),
                Eng(self.q["scalar"], s_ac), Eng(self.q["vector"], s_ve),
                Eng(self.q["tensor"]))

    def sem(self, name):
        return Ctr(self.ctx.enter_context(self.nc.semaphore(name)))

    def sb(self, name, shape, dt):
        return self.ctx.enter_context(self.nc.sbuf_tensor(name, shape, dt))

    def ps(self, name, shape, dt=F32):
        return self.ctx.enter_context(self.nc.psum_tensor(name, shape, dt))

    def run(self):
        def replay(eng, ops):
            for op in ops:
                if op.name == "__wait__":
                    ctr, v = op.args
                    eng.wait_ge(ctr.h, v)
                    continue
                if op.self_wait is not None and op.self_wait[1] > 0:
                    eng.wait_ge(op.self_wait[0].h, op.self_wait[1])
                ins = getattr(eng, op.name)(*op.args, **op.kwargs)
                for (c, k) in op.incs:
                    ins.then_inc(c.h, k)

        with self.nc.Block() as block:
            for e in self.ENGS:
                ops = self.q[e]
                if ops:
                    getattr(block, e)(lambda h, ops=tuple(ops): replay(h, ops))



def ln_stat_tiles(p, tag, width=1024):
    v = p.sb(f"{tag}_v", [1, width], F32)
    return {
        "m": p.sb(f"{tag}_m", [1, width], F32)[:, :],
        "s": p.sb(f"{tag}_s", [1, width], F32)[:, :],
        "v": v[:, :],
        "r": p.sb(f"{tag}_r", [1, width], F32R)[:, :],
        "nmr": p.sb(f"{tag}_nmr", [1, width], F32R)[:, :],
        "mr": v[:, :],
    }


def emit_ln_factors(E, PS, st, T, onesr, s_mm, s_ve, s_ac):
    Y, G, A, V, TE = E
    W = T
    V.wait(s_mm)
    V.tensor_scalar_mul(st["m"][:, :W], PS[0:1, 0:W], 1.0 / D)
    V.tensor_scalar(st["s"][:, :W], PS[0:1, 1024:1024 + W], 1.0 / D, LN_EPS,
                    ALU.mult, ALU.add)
    V.tensor_mul(st["v"][:, :W], st["m"][:, :W], st["m"][:, :W])
    V.tensor_sub(st["v"][:, :W], st["s"][:, :W], st["v"][:, :W])
    A.wait(s_ve)
    A.activation(st["s"][:, :W], st["v"][:, :W], AF.Ln)
    A.activation(st["r"][:, :W], st["s"][:, :W], AF.Exp, scale=-0.5)
    V.wait(s_ac)
    V.tensor_mul(st["mr"][:, :W], st["m"][:, :W], st["r"][:, :W])
    A.wait(s_ve)
    A.mul(st["nmr"][:, :W], st["mr"][:, :W], -1.0)
    TE.wait(s_ac)
    for off in range(0, W, 512):
        n = min(512, W - off)
        TE.matmul(PS[:, 2048 + off:2048 + off + n], onesr[:],
                  st["r"][0:1, off:off + n], start=True, stop=True)
        op = TE.matmul(PS[:, 3072 + off:3072 + off + n], onesr[:],
                       st["nmr"][0:1, off:off + n], start=True, stop=True)
    op.inc(s_mm)



def build_launch1():
    nc = bass.Bass("TRN2", target_bir_lowering=False)
    xin = nc.dram_tensor("xin", [D, TOK], F32, kind="ExternalInput").ap()
    wg = nc.dram_tensor("wg", [4, NG, D, 128], F32, kind="ExternalInput").ap()
    wu = nc.dram_tensor("wu", [4, NG, D, 128], F32, kind="ExternalInput").ap()
    wd = nc.dram_tensor("wd", [4, D, D], F32, kind="ExternalInput").ap()
    onesc_in = nc.dram_tensor("onesc", [128, 1], F32, kind="ExternalInput").ap()
    onesr_in = nc.dram_tensor("onesr", [1, 128], F32, kind="ExternalInput").ap()

    xout = nc.dram_tensor("xout", [D, TOK], F32, kind="ExternalOutput").ap()
    xnout = nc.dram_tensor("xnout", [D, TOK], BF16, kind="ExternalOutput").ap()

    xin_r = xin.bitcast(F32R).rearrange("(kc p) t -> kc p t", p=128)
    xout_r = xout.bitcast(F32R).rearrange("(kc p) t -> kc p t", p=128)
    xnout_r = xnout.rearrange("(kc p) t -> kc p t", p=128)

    with ExitStack() as ctx:
        p = Prog(nc, ctx)
        s_ac = p.sem("s_ac")
        s_ve = p.sem("s_ve")
        E = p.engs(s_ac, s_ve)
        Y, G, A, V, T = E

        acc = p.sb("acc", [128, KC * 1024], F32R)
        xn = p.sb("xn", [128, KC * 1024], BF16)
        onescb = p.sb("onescb", [128, 1], BF16)
        onesc = p.sb("onesc_s", [128, 1], F32R)
        onesr = p.sb("onesr_s", [1, 128], F32R)
        wgs = [p.sb(f"wgs{i}", [128, KC, 128], BF16) for i in range(2)]
        wus = [p.sb(f"wus{i}", [128, KC, 128], BF16) for i in range(2)]
        wds = [[p.sb(f"wds{i}_{m}", [128, 1024], BF16) for m in range(4)]
               for i in range(2)]
        h_t = [[p.sb(f"h_{m}_{t}", [128, 512], F32) for t in range(2)]
               for m in range(4)]
        a_t = [[p.sb(f"a_{m}_{t}", [128, 512], F32) for t in range(2)]
               for m in range(4)]
        thr_t = p.sb("thr", [128, 512], F32)
        thr = [thr_t, thr_t]
        tt1 = p.sb("tt1", [128, 512], F32)
        tt2 = p.sb("tt2", [128, 512], F32)
        hsp = [[p.sb(f"hsp_{m}_{t}", [128, 512], BF16) for t in range(2)]
               for m in range(4)]
        sq = [hsp[0][0], hsp[1][0]]
        st1 = ln_stat_tiles(p, "ln1")
        st2 = st1
        PS = p.ps("PS", [128, 4096], F32)

        s_xp = [p.sem("s_x0"), p.sem("s_x1")]
        s_wg = [p.sem("s_wg0"), p.sem("s_wg1")]
        s_wdp = [p.sem("s_wd0"), p.sem("s_wd1")]
        s_cn = p.sem("s_cn")
        s_mm = p.sem("s_mm")
        s_mf = p.sem("s_mf")
        s_pd = p.sem("s_pd")
        s_st = p.sem("s_st")

        G.dma_start(out=onesc[:], in_=onesc_in.bitcast(F32R)).inc(s_cn, 16)
        G.dma_start(out=onescb[:], in_=onesc_in[:]).inc(s_cn, 16)
        G.dma_start(out=onesr[:], in_=onesr_in.bitcast(F32R)).inc(s_cn, 16)

        ln1_mm0 = s_mm.v
        for kc in range(KC):
            if kc >= 2:
                Y.wait(s_mm, ln1_mm0 + kc - 1)
            Y.dma_start(out=acc[:, kc * 1024:(kc + 1) * 1024],
                        in_=xin_r[kc]).inc(s_xp[kc % 2], 16)
            ld = s_xp[kc % 2].v
            A.wait(s_xp[kc % 2], ld)
            if kc >= 1:
                A.wait(s_mm, ln1_mm0 + kc)
            A.activation(sq[0][:], acc[:, kc * 1024:kc * 1024 + 512], AF.Square)
            A.activation(sq[1][:], acc[:, kc * 1024 + 512:(kc + 1) * 1024],
                         AF.Square)
            sqd = s_ac.v
            T.wait(s_ac, sqd)
            if kc == 0:
                T.wait(s_cn, 48)
            fi, la = (kc == 0), (kc == KC - 1)
            T.matmul(PS[0:1, 0:512], onesc[:],
                     acc[:, kc * 1024:kc * 1024 + 512], start=fi, stop=la)
            T.matmul(PS[0:1, 512:1024], onesc[:],
                     acc[:, kc * 1024 + 512:(kc + 1) * 1024], start=fi, stop=la)
            T.matmul(PS[0:1, 1024:1536], onescb[:], sq[0][:], start=fi, stop=la)
            T.matmul(PS[0:1, 1536:2048], onescb[:], sq[1][:], start=fi, stop=la).inc(s_mm)

        emit_ln_factors(E, PS, st1, 1024, onesr, s_mm, s_ve, s_ac)
        rep1 = s_mm.v

        for kc in range(KC):
            V.wait(s_mm, rep1)
            V.tensor_mul(h_t[0][0][:], acc[:, kc * 1024:kc * 1024 + 512],
                         PS[:, 2048:2560])
            V.tensor_mul(h_t[1][0][:], acc[:, kc * 1024 + 512:(kc + 1) * 1024],
                         PS[:, 2560:3072])
            V.tensor_add(xn[:, kc * 1024:kc * 1024 + 512], h_t[0][0][:],
                         PS[:, 3072:3584])
            V.tensor_add(xn[:, kc * 1024 + 512:kc * 1024 + 1024], h_t[1][0][:],
                         PS[:, 3584:4096])
        xn_done = s_ve.v

        pd_bank_free = {6: 0, 7: 0}
        for gc in range(NG):
            for m in range(4):
                u = gc * 4 + m
                if u >= 2:
                    G.wait(s_mf, 2 * (u - 1))
                G.dma_start(out=wgs[u % 2][:, :, :],
                            in_=wg[m, gc].rearrange("(kc p) mm -> p kc mm", p=128)).inc(s_wg[u % 2], 16)
                G.dma_start(out=wus[u % 2][:, :, :],
                            in_=wu[m, gc].rearrange("(kc p) mm -> p kc mm", p=128)).inc(s_wg[u % 2], 16)
            for half in range(2):
                gh = gc * 2 + half
                if gh >= 2:
                    G.wait(s_pd, 16 * (gh - 1))
                for m in range(4):
                    G.dma_start(
                        out=wds[gh % 2][m][:],
                        in_=wd[m, gc * 128:(gc + 1) * 128,
                               half * 1024:(half + 1) * 1024]).inc(s_wdp[gh % 2], 16)

        for gc in range(NG):
            for m in range(4):
                u = gc * 4 + m
                slot = u % 2
                par = u % 2
                gbank = [par * 4, par * 4 + 2]
                ubank = [par * 4 + 1, par * 4 + 3]
                T.wait(s_wg[u % 2], 32 * (u // 2 + 1))
                if u == 0:
                    T.wait(s_ve, xn_done)
                else:
                    T.wait(s_ve)
                for nt in range(2):
                    gb, ub = gbank[nt], ubank[nt]
                    for kc in range(KC):
                        T.matmul(PS[:, gb * 512:(gb + 1) * 512], wgs[slot][:, kc, :],
                                 xn[:, kc * 1024 + nt * 512: kc * 1024 + (nt + 1) * 512],
                                 start=(kc == 0), stop=(kc == KC - 1))
                    for kc in range(KC):
                        lastmm = T.matmul(
                            PS[:, ub * 512:(ub + 1) * 512], wus[slot][:, kc, :],
                            xn[:, kc * 1024 + nt * 512: kc * 1024 + (nt + 1) * 512],
                            start=(kc == 0), stop=(kc == KC - 1))
                    lastmm.inc(s_mf)
                    mf = s_mf.v
                    A.wait(s_mf, mf)
                    A.activation(h_t[m][nt][:], PS[:, gb * 512:(gb + 1) * 512],
                                 AF.Silu)
                    acd = s_ac.v
                    V.wait(s_ac, acd)
                    V.tensor_mul(h_t[m][nt][:], h_t[m][nt][:],
                                 PS[:, ub * 512:(ub + 1) * 512])
                    ved = s_ve.v
                    A.wait(s_ve, ved)
                    A.activation(a_t[m][nt][:], h_t[m][nt][:], AF.Abs)

            for nt in range(2):
                V.wait(s_ac)
                V.tensor_tensor(thr[nt][:], a_t[0][nt][:], a_t[1][nt][:], ALU.max)
                V.tensor_tensor(tt2[:], a_t[2][nt][:], a_t[3][nt][:], ALU.max)
                V.tensor_tensor(thr[nt][:], thr[nt][:], tt2[:], ALU.min)
                V.tensor_tensor(tt1[:], a_t[0][nt][:], a_t[1][nt][:], ALU.min)
                V.tensor_tensor(tt2[:], a_t[2][nt][:], a_t[3][nt][:], ALU.min)
                V.tensor_tensor(tt1[:], tt1[:], tt2[:], ALU.max)
                V.tensor_tensor(thr[nt][:], thr[nt][:], tt1[:], ALU.max)
                for m in range(4):
                    V.tensor_tensor(a_t[m][nt][:], a_t[m][nt][:], thr[nt][:], ALU.is_ge)
                    V.tensor_mul(hsp[m][nt][:], h_t[m][nt][:], a_t[m][nt][:])
            spd = s_ve.v

            T.wait(s_ve, spd)
            for half in range(2):
                gh = gc * 2 + half
                T.wait(s_wdp[gh % 2], 64 * (gh // 2 + 1))
                for nt in range(2):
                    for j, oc in enumerate(range(half * 8, half * 8 + 8)):
                        pdb = 6 if (j % 2 == 0) else 7
                        if pd_bank_free[pdb] > 0:
                            T.wait(s_ve, pd_bank_free[pdb])
                        for m in range(4):
                            dmm = T.matmul(PS[:, pdb * 512:(pdb + 1) * 512],
                                           wds[gh % 2][m][:, (oc - half * 8) * 128:
                                                          (oc - half * 8 + 1) * 128],
                                           hsp[m][nt][:], start=(m == 0), stop=(m == 3))
                        dmm.inc(s_pd)
                        pdv = s_pd.v
                        V.wait(s_pd, pdv)
                        V.scalar_tensor_tensor(
                            acc[:, oc * 1024 + nt * 512: oc * 1024 + (nt + 1) * 512],
                            PS[:, pdb * 512:(pdb + 1) * 512], 0.5,
                            acc[:, oc * 1024 + nt * 512: oc * 1024 + (nt + 1) * 512],
                            ALU.mult, ALU.add)
                        pd_bank_free[pdb] = s_ve.v

        ffn_done_ve = s_ve.v

        ln2_mm0 = s_mm.v
        for kc in range(KC):
            A.wait(s_ve, ffn_done_ve)
            if kc >= 1:
                A.wait(s_mm, ln2_mm0 + kc)
            A.activation(sq[0][:], acc[:, kc * 1024:kc * 1024 + 512], AF.Square)
            A.activation(sq[1][:], acc[:, kc * 1024 + 512:(kc + 1) * 1024],
                         AF.Square)
            acd = s_ac.v
            T.wait(s_ac, acd)
            fi, la = (kc == 0), (kc == KC - 1)
            T.matmul(PS[0:1, 0:512], onesc[:],
                     acc[:, kc * 1024:kc * 1024 + 512], start=fi, stop=la)
            T.matmul(PS[0:1, 512:1024], onesc[:],
                     acc[:, kc * 1024 + 512:(kc + 1) * 1024], start=fi, stop=la)
            T.matmul(PS[0:1, 1024:1536], onescb[:], sq[0][:], start=fi, stop=la)
            T.matmul(PS[0:1, 1536:2048], onescb[:], sq[1][:], start=fi, stop=la).inc(s_mm)
            Y.wait(s_ve, ffn_done_ve)
            Y.dma_start(out=xout_r[kc], in_=acc[:, kc * 1024:(kc + 1) * 1024]).inc(s_st, 16)

        emit_ln_factors(E, PS, st2, 1024, onesr, s_mm, s_ve, s_ac)
        rep2 = s_mm.v

        for kc in range(KC):
            V.wait(s_mm, rep2)
            V.tensor_mul(h_t[0][0][:], acc[:, kc * 1024:kc * 1024 + 512],
                         PS[:, 2048:2560])
            V.tensor_mul(h_t[1][0][:], acc[:, kc * 1024 + 512:(kc + 1) * 1024],
                         PS[:, 2560:3072])
            V.tensor_add(xn[:, kc * 1024:kc * 1024 + 512], h_t[0][0][:],
                         PS[:, 3072:3584])
            V.tensor_add(xn[:, kc * 1024 + 512:kc * 1024 + 1024], h_t[1][0][:],
                         PS[:, 3584:4096])
            ved = s_ve.v
            Y.wait(s_ve, ved)
            Y.dma_start(out=xnout_r[kc], in_=xn[:, kc * 1024:(kc + 1) * 1024]).inc(s_st, 16)

        Y.wait(s_st, s_st.v)
        p.run()
    return nc



def build_wout():
    nc = bass.Bass("TRN2", target_bir_lowering=False)
    xin = nc.dram_tensor("xin", [D, TOK], F32, kind="ExternalInput").ap()
    oin = nc.dram_tensor("oin", [D, TOK], BF16, kind="ExternalInput").ap()
    wo = nc.dram_tensor("wo", [D, D], F32, kind="ExternalInput").ap()
    onesc_in = nc.dram_tensor("onesc", [128, 1], F32, kind="ExternalInput").ap()
    onesr_in = nc.dram_tensor("onesr", [1, 128], F32, kind="ExternalInput").ap()

    xout = nc.dram_tensor("xout", [D, TOK], F32, kind="ExternalOutput").ap()
    xnout = nc.dram_tensor("xnout", [D, TOK], BF16, kind="ExternalOutput").ap()

    xin_r = xin.bitcast(F32R).rearrange("(kc p) t -> kc p t", p=128)
    oin_r = oin.rearrange("(kc p) t -> kc p t", p=128)
    xout_r = xout.bitcast(F32R).rearrange("(kc p) t -> kc p t", p=128)
    xnout_r = xnout.rearrange("(kc p) t -> kc p t", p=128)

    with ExitStack() as ctx:
        p = Prog(nc, ctx)
        s_ac = p.sem("s_ac")
        s_ve = p.sem("s_ve")
        E = p.engs(s_ac, s_ve)
        Y, G, A, V, T = E

        acc = p.sb("acc", [128, KC * 1024], F32R)
        o_sb = p.sb("o_sb", [128, KC * 1024], BF16)
        xn = p.sb("xn", [128, KC * 1024], BF16)
        wo_st = [p.sb(f"wo{i}", [128, D], BF16) for i in range(2)]
        sq = [p.sb(f"sq{i}", [128, 512], BF16) for i in range(2)]
        onescb = p.sb("onescb", [128, 1], BF16)
        onesc = p.sb("onesc_s", [128, 1], F32R)
        onesr = p.sb("onesr_s", [1, 128], F32R)
        st1 = ln_stat_tiles(p, "ln1")
        PS = p.ps("PS", [128, 4096], F32)

        s_x = [p.sem("s_x0"), p.sem("s_x1")]
        s_o = [p.sem("s_o0"), p.sem("s_o1")]
        s_w = [p.sem("s_w0"), p.sem("s_w1")]
        s_cn = p.sem("s_cn")
        s_mm = p.sem("s_mm")
        s_pd = p.sem("s_pd")
        s_st = p.sem("s_st")

        G.dma_start(out=onesc[:], in_=onesc_in.bitcast(F32R)).inc(s_cn, 16)
        G.dma_start(out=onescb[:], in_=onesc_in[:]).inc(s_cn, 16)
        G.dma_start(out=onesr[:], in_=onesr_in.bitcast(F32R)).inc(s_cn, 16)

        for kc in range(KC):
            Y.dma_start(out=acc[:, kc * 1024:(kc + 1) * 1024],
                        in_=xin_r[kc]).inc(s_x[kc % 2], 16)
            Y.dma_start(out=o_sb[:, kc * 1024:(kc + 1) * 1024],
                        in_=oin_r[kc]).inc(s_o[kc % 2], 16)
        x_all = (s_x[0].v, s_x[1].v)
        o_all = (s_o[0].v, s_o[1].v)

        for kc in range(KC):
            if kc >= 2:
                G.wait(s_mm, kc - 1)
            G.dma_start(out=wo_st[kc % 2][:],
                        in_=wo[kc * 128:(kc + 1) * 128, :]).inc(s_w[kc % 2], 16)

        pass
    return None


def build_wout2():
    nc = bass.Bass("TRN2", target_bir_lowering=False)
    xin = nc.dram_tensor("xin", [D, TOK], F32, kind="ExternalInput").ap()
    oin = nc.dram_tensor("oin", [D, TOK], BF16, kind="ExternalInput").ap()
    wo = nc.dram_tensor("wo", [D, D], F32, kind="ExternalInput").ap()
    onesc_in = nc.dram_tensor("onesc", [128, 1], F32, kind="ExternalInput").ap()
    onesr_in = nc.dram_tensor("onesr", [1, 128], F32, kind="ExternalInput").ap()

    xout = nc.dram_tensor("xout", [D, TOK], F32, kind="ExternalOutput").ap()
    xnout = nc.dram_tensor("xnout", [D, TOK], BF16, kind="ExternalOutput").ap()

    xin_r = xin.bitcast(F32R).rearrange("(kc p) t -> kc p t", p=128)
    oin_r = oin.rearrange("(kc p) t -> kc p t", p=128)
    xout_r = xout.bitcast(F32R).rearrange("(kc p) t -> kc p t", p=128)
    xnout_r = xnout.rearrange("(kc p) t -> kc p t", p=128)

    with ExitStack() as ctx:
        p = Prog(nc, ctx)
        s_ac = p.sem("s_ac")
        s_ve = p.sem("s_ve")
        E = p.engs(s_ac, s_ve)
        Y, G, A, V, T = E

        acc = p.sb("acc", [128, KC * 1024], F32R)
        o_sb = p.sb("o_sb", [128, KC * 1024], BF16)
        xn = o_sb
        wo_st = [[p.sb(f"wo{g}_{j}", [128, D], BF16) for j in range(8)]
                 for g in range(2)]
        sq = [p.sb(f"sq{i}", [128, 512], BF16) for i in range(2)]
        onescb = p.sb("onescb", [128, 1], BF16)
        onesc = p.sb("onesc_s", [128, 1], F32R)
        onesr = p.sb("onesr_s", [1, 128], F32R)
        st1 = ln_stat_tiles(p, "ln1")
        PS = p.ps("PS", [128, 4096], F32)

        s_x = [p.sem("s_x0"), p.sem("s_x1")]
        s_o = [p.sem("s_o0"), p.sem("s_o1")]
        s_w = [p.sem("s_w0"), p.sem("s_w1")]
        s_cn = p.sem("s_cn")
        s_mm = p.sem("s_mm")
        s_st = p.sem("s_st")

        G.dma_start(out=onesc[:], in_=onesc_in.bitcast(F32R)).inc(s_cn, 16)
        G.dma_start(out=onescb[:], in_=onesc_in[:]).inc(s_cn, 16)
        G.dma_start(out=onesr[:], in_=onesr_in.bitcast(F32R)).inc(s_cn, 16)

        for kc in range(KC):
            Y.dma_start(out=acc[:, kc * 1024:(kc + 1) * 1024],
                        in_=xin_r[kc]).inc(s_x[kc % 2], 16)
            Y.dma_start(out=o_sb[:, kc * 1024:(kc + 1) * 1024],
                        in_=oin_r[kc]).inc(s_o[kc % 2], 16)

        for g in range(2):
            if g == 1:
                G.wait(s_mm, 32)
            for j in range(8):
                G.dma_start(out=wo_st[g][j][:],
                            in_=wo[(g * 8 + j) * 128:(g * 8 + j + 1) * 128, :]).inc(s_w[g], 16)

        for g in range(2):
            T.wait(s_w[g], 16 * 8)
            if g == 0:
                T.wait(s_o[0], s_o[0].v)
                T.wait(s_o[1], s_o[1].v)
                T.wait(s_cn, 48)
            for mc in range(KC):
                for nt in range(2):
                    bank = (mc * 2 + nt) % 2
                    if g + mc + nt > 0:
                        step = g * 32 + mc * 2 + nt
                        if step >= 2:
                            T.wait(s_ve, ve_base + step - 1)
                    for j in range(8):
                        kc = g * 8 + j
                        mmop = T.matmul(
                            PS[:, bank * 512:(bank + 1) * 512],
                            wo_st[g][j][:, mc * 128:(mc + 1) * 128],
                            o_sb[:, kc * 1024 + nt * 512: kc * 1024 + (nt + 1) * 512],
                            start=(j == 0), stop=(j == 7))
                    mmop.inc(s_mm)
                    mmv = s_mm.v
                    if g == 0 and mc == 0 and nt == 0:
                        ve_base = s_ve.v
                        V.wait(s_x[0], s_x[0].v)
                        V.wait(s_x[1], s_x[1].v)
                    V.wait(s_mm, mmv)
                    V.tensor_add(
                        acc[:, mc * 1024 + nt * 512: mc * 1024 + (nt + 1) * 512],
                        acc[:, mc * 1024 + nt * 512: mc * 1024 + (nt + 1) * 512],
                        PS[:, bank * 512:(bank + 1) * 512])
        res_done = s_ve.v

        emit_ln_resident(p, E, PS, st1, acc, xn, sq, onesc, onescb, onesr,
                         s_mm, s_ve, s_ac, s_st, xout_r, xnout_r, res_done, Y)
        Y.wait(s_st, s_st.v)
        p.run()
    return nc


def emit_ln_resident(p, E, PS, st, acc, xn, sq, onesc, onescb, onesr,
                     s_mm, s_ve, s_ac, s_st, xout_r, xnout_r, ready_ve, Y):
    _, G, A, V, T = E
    mm0 = s_mm.v
    for kc in range(KC):
        A.wait(s_ve, ready_ve)
        if kc >= 1:
            A.wait(s_mm, mm0 + kc)
        A.activation(sq[0][:], acc[:, kc * 1024:kc * 1024 + 512], AF.Square)
        A.activation(sq[1][:], acc[:, kc * 1024 + 512:(kc + 1) * 1024], AF.Square)
        acd = s_ac.v
        T.wait(s_ac, acd)
        fi, la = (kc == 0), (kc == KC - 1)
        T.matmul(PS[0:1, 0:512], onesc[:],
                 acc[:, kc * 1024:kc * 1024 + 512], start=fi, stop=la)
        T.matmul(PS[0:1, 512:1024], onesc[:],
                 acc[:, kc * 1024 + 512:(kc + 1) * 1024], start=fi, stop=la)
        T.matmul(PS[0:1, 1024:1536], onescb[:], sq[0][:], start=fi, stop=la)
        T.matmul(PS[0:1, 1536:2048], onescb[:], sq[1][:], start=fi, stop=la).inc(s_mm)
        if xout_r is not None:
            Y.wait(s_ve, ready_ve)
            Y.dma_start(out=xout_r[kc],
                        in_=acc[:, kc * 1024:(kc + 1) * 1024]).inc(s_st, 16)

    emit_ln_factors(E, PS, st, 1024, onesr, s_mm, s_ve, s_ac)
    rep = s_mm.v

    if xnout_r is not None:
        for kc in range(KC):
            V.wait(s_mm, rep)
            V.tensor_mul(sq[0][:].bitcast(F32) if False else xn[:, kc * 1024:kc * 1024 + 512],
                         acc[:, kc * 1024:kc * 1024 + 512], PS[:, 2048:2560])
            V.tensor_mul(xn[:, kc * 1024 + 512:(kc + 1) * 1024],
                         acc[:, kc * 1024 + 512:(kc + 1) * 1024], PS[:, 2560:3072])
            V.tensor_tensor(xn[:, kc * 1024:kc * 1024 + 512],
                            xn[:, kc * 1024:kc * 1024 + 512], PS[:, 3072:3584],
                            ALU.add)
            V.tensor_tensor(xn[:, kc * 1024 + 512:(kc + 1) * 1024],
                            xn[:, kc * 1024 + 512:(kc + 1) * 1024], PS[:, 3584:4096],
                            ALU.add)
            ved = s_ve.v
            Y.wait(s_ve, ved)
            Y.dma_start(out=xnout_r[kc],
                        in_=xn[:, kc * 1024:(kc + 1) * 1024]).inc(s_st, 16)



def build_ffn1(final_ln):
    nc = bass.Bass("TRN2", target_bir_lowering=False)
    xin = nc.dram_tensor("xin", [D, TOK], F32, kind="ExternalInput").ap()
    xnin = nc.dram_tensor("xnin", [D, TOK], BF16, kind="ExternalInput").ap()
    w1 = nc.dram_tensor("w1", [D, D], F32, kind="ExternalInput").ap()
    scl = nc.dram_tensor("scl", [128, KC], F32, kind="ExternalInput").ap()
    onesc_in = nc.dram_tensor("onesc", [128, 1], F32, kind="ExternalInput").ap()
    onesr_in = nc.dram_tensor("onesr", [1, 128], F32, kind="ExternalInput").ap()

    xout = nc.dram_tensor("xout", [D, TOK], F32, kind="ExternalOutput").ap()
    if final_ln:
        xnout = nc.dram_tensor("xnout", [D, TOK], BF16, kind="ExternalOutput").ap()
        xnout_r = xnout.rearrange("(kc p) t -> kc p t", p=128)
    else:
        xnout_r = None
    oscr = nc.dram_tensor("oscr", [D, TOK], BF16, kind="Internal").ap()

    xin_r = xin.bitcast(F32R).rearrange("(kc p) t -> kc p t", p=128)
    xnin_r = xnin.rearrange("(kc p) t -> kc p t", p=128)
    xout_r = xout.bitcast(F32R).rearrange("(kc p) t -> kc p t", p=128)
    oscr_r = oscr.rearrange("(kc p) t -> kc p t", p=128)

    with ExitStack() as ctx:
        p = Prog(nc, ctx)
        s_ac = p.sem("s_ac")
        s_ve = p.sem("s_ve")
        E = p.engs(s_ac, s_ve)
        Y, G, A, V, T = E

        acc = p.sb("acc", [128, KC * 1024], F32R)
        xn = p.sb("xn", [128, KC * 1024], BF16)
        w1s = [p.sb(f"w1s{j}", [128, D], BF16) for j in range(KC)]
        scl_s = p.sb("scl_s", [128, KC], F32)
        os_t = [p.sb(f"os{i}", [128, 512], BF16) for i in range(2)]
        sqb = [p.sb(f"sqb{i}", [128, 512], BF16) for i in range(2)]
        tmp = [p.sb(f"tmp{i}", [128, 512], F32) for i in range(2)]
        sq = sqb
        onescb = p.sb("onescb", [128, 1], BF16)
        onesc = p.sb("onesc_s", [128, 1], F32R)
        onesr = p.sb("onesr_s", [1, 128], F32R)
        st1 = ln_stat_tiles(p, "ln1")
        rowA, rowB, rowC = st1["m"], st1["s"], st1["v"]
        rowD, rowF = st1["m"], st1["nmr"]
        PS = p.ps("PS", [128, 4096], F32)

        s_x = [p.sem("s_x0"), p.sem("s_x1")]
        s_n = [p.sem("s_n0"), p.sem("s_n1")]
        s_w = [p.sem("s_w0"), p.sem("s_w1")]
        s_cn = p.sem("s_cn")
        s_mm = p.sem("s_mm")
        s_st = p.sem("s_st")
        s_ob = [p.sem("s_ob0"), p.sem("s_ob1")]
        s_os = [p.sem("s_os0"), p.sem("s_os1")]

        G.dma_start(out=onesc[:], in_=onesc_in.bitcast(F32R)).inc(s_cn, 16)
        G.dma_start(out=onescb[:], in_=onesc_in[:]).inc(s_cn, 16)
        G.dma_start(out=onesr[:], in_=onesr_in.bitcast(F32R)).inc(s_cn, 16)
        G.dma_start(out=scl_s[:], in_=scl[:]).inc(s_cn, 16)

        for kc in range(KC):
            Y.dma_start(out=acc[:, kc * 1024:(kc + 1) * 1024],
                        in_=xin_r[kc]).inc(s_x[kc % 2], 16)
            Y.dma_start(out=xn[:, kc * 1024:(kc + 1) * 1024],
                        in_=xnin_r[kc]).inc(s_n[kc % 2], 16)
        for kc in range(KC):
            G.dma_start(out=w1s[kc][:],
                        in_=w1[kc * 128:(kc + 1) * 128, :]).inc(s_w[kc % 2], 16)

        T.wait(s_w[0], s_w[0].v)
        T.wait(s_w[1], s_w[1].v)
        T.wait(s_n[0], s_n[0].v)
        T.wait(s_n[1], s_n[1].v)
        T.wait(s_cn, 64)
        for mc in range(KC):
            for nt in range(2):
                step = mc * 2 + nt
                bank = step % 2
                if step >= 2:
                    T.wait(s_ac, ac_base + (step - 1))
                for kc in range(KC):
                    mmop = T.matmul(
                        PS[:, bank * 512:(bank + 1) * 512],
                        w1s[kc][:, mc * 128:(mc + 1) * 128],
                        xn[:, kc * 1024 + nt * 512: kc * 1024 + (nt + 1) * 512],
                        start=(kc == 0), stop=(kc == KC - 1))
                mmop.inc(s_mm)
                mmv = s_mm.v
                V.wait(s_mm, mmv)
                if step >= 2:
                    V.wait(s_os[bank], 16 * (step // 2))
                V.tensor_scalar_mul(os_t[bank][:], PS[:, bank * 512:(bank + 1) * 512],
                                    scl_s[:, mc:mc + 1])
                osv = s_ve.v
                A.wait(s_ve, osv)
                A.activation(sqb[bank][:], os_t[bank][:], AF.Square)
                if step == 0:
                    ac_base = s_ac.v - 1
                sqv = s_ac.v
                T.wait(s_ac, sqv)
                T.matmul(PS[0:1, 2048 + nt * 512:2048 + (nt + 1) * 512],
                         onescb[:], sqb[bank][:],
                         start=(mc == 0), stop=(mc == KC - 1)).inc(s_mm)
                Y.wait(s_ve, osv)
                Y.dma_start(out=oscr_r[mc][:, nt * 512:(nt + 1) * 512],
                            in_=os_t[bank][:]).inc(s_os[bank], 16)
        ss_mm = s_mm.v

        V.wait(s_mm, ss_mm)
        V.tensor_copy(rowA[:], PS[0:1, 2048:3072])
        A.wait(s_ve)
        A.activation(rowB[:], rowA[:], AF.Ln)
        A.activation(rowB[:], rowB[:], AF.Exp, scale=0.5)
        V.wait(s_ac)
        V.tensor_scalar_max(rowB[:], rowB[:], 1e-7)
        A.wait(s_ve)
        A.activation(rowC[:], rowB[:], AF.Tanh)
        V.wait(s_ac)
        V.tensor_scalar_add(rowD[:], rowB[:], 1e-7)
        V.reciprocal(rowD[:], rowD[:])
        V.tensor_mul(rowC[:], rowC[:], rowD[:])
        V.tensor_mul(rowB[:], rowB[:], rowC[:])
        V.tensor_scalar_max(rowB[:], rowB[:], 1e-7)
        V.reciprocal(rowB[:], rowB[:])
        V.tensor_scalar(rowB[:], rowB[:], 0.99, 1.0, ALU.mult, ALU.min)
        V.tensor_mul(rowC[:], rowC[:], rowB[:])
        V.tensor_scalar_mul(rowF[:], rowC[:], 0.5)
        fv = s_ve.v
        T.wait(s_ve, fv)
        T.matmul(PS[:, 2048:2560], onesr[:], rowF[0:1, 0:512],
                 start=True, stop=True)
        T.matmul(PS[:, 2560:3072], onesr[:], rowF[0:1, 512:1024],
                 start=True, stop=True).inc(s_mm)
        frep = s_mm.v

        Y.wait(s_os[0], 16 * 16)
        Y.wait(s_os[1], 16 * 16)
        Y.wait(s_ac)
        res0 = s_ve.v
        for kc in range(KC):
            for nt in range(2):
                step = kc * 2 + nt
                buf = step % 2
                if step >= 2:
                    Y.wait(s_ve, res0 + (step - 1) * 2)
                Y.dma_start(out=os_t[buf][:],
                            in_=oscr_r[kc][:, nt * 512:(nt + 1) * 512]).inc(s_ob[buf], 16)
                obv = s_ob[buf].v
                V.wait(s_mm, frep)
                V.wait(s_ob[buf], obv)
                V.wait(s_x[0], 16 * 8)
                V.wait(s_x[1], 16 * 8)
                V.tensor_mul(tmp[buf][:], os_t[buf][:],
                             PS[:, 2048 + nt * 512:2048 + (nt + 1) * 512])
                V.tensor_add(acc[:, kc * 1024 + nt * 512: kc * 1024 + (nt + 1) * 512],
                             acc[:, kc * 1024 + nt * 512: kc * 1024 + (nt + 1) * 512],
                             tmp[buf][:])
        res_done = s_ve.v

        if final_ln:
            emit_ln_resident(p, E, PS, st1, acc, xn, sq, onesc, onescb, onesr,
                             s_mm, s_ve, s_ac, s_st, xout_r, xnout_r, res_done, Y)
        else:
            for kc in range(KC):
                Y.wait(s_ve, res_done)
                Y.dma_start(out=xout_r[kc],
                            in_=acc[:, kc * 1024:(kc + 1) * 1024]).inc(s_st, 16)
        Y.wait(s_st, s_st.v)
        p.run()
    return nc




def _spans512(a, b):
    out = []
    while a < b:
        n = min(b, (a // 512 + 1) * 512) - a
        out.append((a, n))
        a += n
    return out

def build_attn0():
    nc = bass.Bass("TRN2", target_bir_lowering=False)
    xnin = nc.dram_tensor("xnin", [D, S], BF16, kind="ExternalInput").ap()
    wq = nc.dram_tensor("wq", [8, D, 128], F32, kind="ExternalInput").ap()
    wk = nc.dram_tensor("wk", [8, D, 128], F32, kind="ExternalInput").ap()
    wv = nc.dram_tensor("wv", [D, 1024], F32, kind="ExternalInput").ap()
    wf = nc.dram_tensor("wf", [D, 16], F32, kind="ExternalInput").ap()
    bfb = nc.dram_tensor("bfb", [1, 16], F32, kind="ExternalInput").ap()
    ones2d_in = nc.dram_tensor("ones2d", [128, 128], F32, kind="ExternalInput").ap()
    triu_in = nc.dram_tensor("triu", [128, 128], F32, kind="ExternalInput").ap()
    onesr_in = nc.dram_tensor("onesr", [1, 128], F32, kind="ExternalInput").ap()
    iden_in = nc.dram_tensor("iden", [128, 128], F32, kind="ExternalInput").ap()
    maskd_in = nc.dram_tensor("maskd", [128, 128], F32, kind="ExternalInput").ap()

    o_out = nc.dram_tensor("o_out", [1024, S], BF16, kind="ExternalOutput").ap()
    o_out_r = o_out.rearrange("(hc p) t -> hc p t", p=128)
    xnin_r = xnin.rearrange("(kc p) t -> kc p t", p=128)

    VW = 1048

    with ExitStack() as ctx:
        p = Prog(nc, ctx)
        s_ac = p.sem("s_ac")
        s_ve = p.sem("s_ve")
        E = p.engs(s_ac, s_ve)
        Y, G, A, V, T = E

        xn = p.sb("xn", [128, KC * S], BF16)
        arena = p.sb("arena", [128, 8 * S], BF16)
        K_sb = p.sb("K_sb", [128, 8 * S], BF16)
        V_tm = p.sb("V_tm", [128, 16 * VW], BF16)
        wqs = [p.sb(f"wqs{i}", [128, KC, 128], BF16) for i in range(2)]
        wks = wqs
        wfs = p.sb("wfs", [128, KC, 16], BF16)
        bfs = p.sb("bfs", [1, 16], BF16)
        ones2d = p.sb("ones2d_s", [128, 128], BF16)
        triu = p.sb("triu_s", [128, 128], BF16)
        onesrb = p.sb("onesrb", [1, 128], BF16)
        onesr = p.sb("onesr_s", [1, 128], F32R)
        iden = p.sb("iden_s", [128, 128], F32)
        maskd = p.sb("maskd_s", [128, 128], F32)
        f_tm = p.sb("f_tm", [128, KC * 16], BF16)
        c_tm = p.sb("c_tm", [128, KC * 16], F32)
        negc = p.sb("negc", [128, KC * 16], F32)
        lsa = p.sb("lsa", [128, 16], F32)
        lsb = p.sb("lsb", [128, 16], F32)
        crow = p.sb("crow", [1, S], F32R)
        crep = p.sb("crep", [128, S], BF16)
        rrow = p.sb("rrow", [1, 1024], F32)[:, :]
        rrowb = p.sb("rrowb", [1, 1024], BF16)
        o_t1 = p.sb("o_t1", [128, 1024], F32)
        o_t = [o_t1, o_t1]
        o_bf = [p.sb(f"o_bf{i}", [128, 1024], BF16) for i in range(2)]
        Pt = [p.sb(f"Pt{i}", [128, 1024], BF16) for i in range(2)]
        PS = p.ps("PS", [128, 4096], F32)

        s_n = [p.sem("s_n0"), p.sem("s_n1")]
        s_w = [p.sem("s_w0"), p.sem("s_w1")]
        s_wu = [p.sem("s_wu0"), p.sem("s_wu1")]
        s_cn = p.sem("s_cn")
        s_mm = p.sem("s_mm")
        s_stp = [p.sem("s_st0"), p.sem("s_st1")]
        ep_cnt = [0]

        G.dma_start(out=onesrb[:], in_=onesr_in[:]).inc(s_cn, 16)
        G.dma_start(out=onesr[:], in_=onesr_in.bitcast(F32R)).inc(s_cn, 16)
        G.dma_start(out=iden[:], in_=iden_in[:]).inc(s_cn, 16)
        G.dma_start(out=maskd[:], in_=maskd_in[:]).inc(s_cn, 16)
        G.dma_start(out=ones2d[:], in_=ones2d_in[:]).inc(s_cn, 16)
        G.dma_start(out=triu[:], in_=triu_in[:]).inc(s_cn, 16)
        G.dma_start(out=bfs[:], in_=bfb[:]).inc(s_cn, 16)
        G.dma_start(out=wfs[:, :, :],
                    in_=wf.rearrange("(kc p) h -> p kc h", p=128)).inc(s_cn, 16)
        cn_all = s_cn.v
        for kc in range(KC):
            Y.dma_start(out=xn[:, kc * S:(kc + 1) * S],
                        in_=xnin_r[kc]).inc(s_n[kc % 2], 16)
        for kc in range(KC):
            G.dma_start(out=arena[:, kc * 1024:(kc + 1) * 1024],
                        in_=wv[kc * 128:(kc + 1) * 128, :]).inc(s_w[kc % 2], 16)
        wv0, wv1 = s_w[0].v, s_w[1].v

        V.memset(V_tm[:], 1.0)
        vm_set = s_ve.v
        T.wait(s_w[0], wv0)
        T.wait(s_w[1], wv1)
        T.wait(s_n[0], s_n[0].v)
        T.wait(s_n[1], s_n[1].v)
        T.wait(s_cn, cn_all)
        T.wait(s_ve, vm_set)
        vevac0 = None
        for tc in range(16):
            for nv in range(2):
                step = tc * 2 + nv
                bank = step % 2
                if step >= 2:
                    T.wait(s_ve, vevac0 + step - 1)
                for kc in range(KC):
                    mmop = T.matmul(
                        PS[:, bank * 512:(bank + 1) * 512],
                        xn[:, kc * S + tc * 128: kc * S + (tc + 1) * 128],
                        arena[:, kc * 1024 + nv * 512: kc * 1024 + (nv + 1) * 512],
                        start=(kc == 0), stop=(kc == KC - 1))
                mmop.inc(s_mm)
                mmv = s_mm.v
                if step == 0:
                    vevac0 = s_ve.v
                V.wait(s_mm, mmv)
                V.tensor_copy(
                    V_tm[:, tc * VW + nv * 520: tc * VW + nv * 520 + 520]
                    .rearrange("p (h c) -> p h c", c=65)[:, :, 0:64],
                    PS[:, bank * 512:(bank + 1) * 512]
                    .rearrange("p (h c) -> p h c", c=64))
        vproj_ve = s_ve.v

        def proj_qk(wblk, stage, dest, scale, base_u, first_wait_ve):
            for mc in range(8):
                u = base_u + mc
                if u >= 2:
                    G.wait(s_mm, unit_mm[u - 2])
                G.dma_start(out=stage[u % 2][:, :, :],
                            in_=wblk[mc].rearrange("(kc p) mm -> p kc mm", p=128)
                            ).inc(s_wu[u % 2], 16)
                w_need = s_wu[u % 2].v
                T.wait(s_wu[u % 2], w_need)
                if u == 0 and first_wait_ve is not None:
                    T.wait(s_ve, first_wait_ve)
                for nt in range(4):
                    step = u * 4 + nt
                    bank = step % 2 + 2
                    if step >= 2:
                        T.wait(s_ac, qk_ac0[0] + step - 1)
                    for kc in range(KC):
                        mmop = T.matmul(
                            PS[:, bank * 512:(bank + 1) * 512],
                            stage[u % 2][:, kc, :],
                            xn[:, kc * S + nt * 512: kc * S + (nt + 1) * 512],
                            start=(kc == 0), stop=(kc == KC - 1))
                    mmop.inc(s_mm)
                    mmv = s_mm.v
                    if step == 0 and qk_ac0[0] is None:
                        qk_ac0[0] = s_ac.v
                    A.wait(s_mm, mmv)
                    if scale is None:
                        A.copy(dest[:, mc * S + nt * 512: mc * S + (nt + 1) * 512],
                               PS[:, bank * 512:(bank + 1) * 512])
                    else:
                        A.mul(dest[:, mc * S + nt * 512: mc * S + (nt + 1) * 512],
                              PS[:, bank * 512:(bank + 1) * 512], scale)
                unit_mm.append(s_mm.v)

        unit_mm = []
        qk_ac0 = [None]
        proj_qk(wq, wqs, arena, 0.125, 0, vproj_ve)
        proj_qk(wk, wks, K_sb, None, 8, None)
        qk_done_ac = s_ac.v

        T.wait(s_ac, qk_done_ac)
        for tc in range(16):
            for kc in range(KC):
                T.matmul(PS[:, 2048:2064],
                         xn[:, kc * S + tc * 128: kc * S + (tc + 1) * 128],
                         wfs[:, kc, :], start=(kc == 0), stop=False)
            fmm = T.matmul(PS[:, 2048:2064], onesrb[:], bfs[:],
                           start=False, stop=True)
            fmm.inc(s_mm)
            fv = s_mm.v
            V.wait(s_mm, fv)
            V.tensor_scalar_min(lsa[:], PS[:, 2048:2064], 0.0)
            A.wait(s_mm, fv)
            A.activation(lsb[:], PS[:, 2048:2064], AF.Abs)
            A.activation(lsb[:], lsb[:], AF.Exp, scale=-1.0)
            A.activation(lsb[:], lsb[:], AF.Ln, bias=1.0)
            lsv = s_ac.v
            V.wait(s_ac, lsv)
            V.tensor_sub(f_tm[:, tc * 16:(tc + 1) * 16], lsa[:], lsb[:])
            fve = s_ve.v
            T.wait(s_ve, fve)
        f_done_ve = s_ve.v
        for tc in range(16):
            for tc2 in range(tc + 1):
                lhs = triu[:] if tc2 == tc else ones2d[:]
                cmm = T.matmul(PS[:, 2560:2576], lhs,
                               f_tm[:, tc2 * 16:(tc2 + 1) * 16],
                               start=(tc2 == 0), stop=(tc2 == tc))
            cmm.inc(s_mm)
            cv = s_mm.v
            V.wait(s_mm, cv)
            V.tensor_copy(c_tm[:, tc * 16:(tc + 1) * 16], PS[:, 2560:2576])
            A.wait(s_mm, cv)
            A.mul(negc[:, tc * 16:(tc + 1) * 16], PS[:, 2560:2576], -1.0)
            cve, cac = s_ve.v, s_ac.v
            T.wait(s_ve, cve)
            T.wait(s_ac, cac)
        c_done = (s_ve.v, s_ac.v)

        for h in range(16):
            hc, hrow = h // 2, (h % 2) * 64
            if h == 0:
                T.wait(s_ve, c_done[0])
                T.wait(s_ac, c_done[1])
            else:
                T.wait(s_ve)
            for tc in range(16):
                ctr = T.transpose(PS[0:1, 2048 + tc * 128:2048 + (tc + 1) * 128],
                                  c_tm[:, tc * 16 + h: tc * 16 + h + 1], iden[:])
            ctr.inc(s_mm)
            crv = s_mm.v
            A.wait(s_mm, crv)
            A.copy(crow[:], PS[0:1, 2048:4096])
            crac = s_ac.v
            T.wait(s_ac, crac)
            for off in range(0, S, 512):
                rmm = T.matmul(PS[:, 2048 + (off % 2048):2048 + (off % 2048) + 512],
                               onesr[:], crow[0:1, off:off + 512],
                               start=True, stop=True)
                rmm.inc(s_mm)
                rv = s_mm.v
                V.wait(s_mm, rv)
                V.tensor_copy(crep[:, off:off + 512],
                              PS[:, 2048 + (off % 2048):2048 + (off % 2048) + 512])
            crep_ve = s_ve.v

            pv_mm = {}
            for qh in range(2):
                nkc = 8 if qh == 0 else 16
                exp_ac = {}
                kcs = list(reversed(range(nkc)))
                for idx, kc in enumerate(kcs):
                    colstart = max(0, kc * 128 - qh * 1024)
                    sb = 0 if kc % 2 == 0 else 3072
                    if idx >= 2:
                        T.wait(s_ac, exp_ac[kcs[idx - 2]])
                    elif qh == 1 or h > 0:
                        T.wait(s_ac)
                    if qh == 0 and idx < 2:
                        T.wait(s_ve, crep_ve)
                    for off, n in _spans512(colstart, 1024):
                        smm = T.matmul(
                            PS[:, sb + off: sb + off + n],
                            K_sb[hrow:hrow + 64, hc * S + kc * 128: hc * S + (kc + 1) * 128],
                            arena[hrow:hrow + 64,
                                  hc * S + qh * 1024 + off: hc * S + qh * 1024 + off + n],
                            start=True, stop=True)
                    smm.inc(s_mm)
                    sv = s_mm.v
                    V.wait(s_mm, sv)
                    if kc * 128 >= qh * 1024:
                        V.tensor_tensor(PS[:, sb + colstart: sb + colstart + 128],
                                        PS[:, sb + colstart: sb + colstart + 128],
                                        maskd[:], ALU.add)
                    V.tensor_tensor(PS[:, sb + colstart: sb + 1024],
                                    PS[:, sb + colstart: sb + 1024],
                                    crep[:, qh * 1024 + colstart:(qh + 1) * 1024],
                                    ALU.add)
                    if idx == 0 and colstart > 0:
                        V.memset(Pt[kc % 2][:, 0:colstart], 0.0)
                    mve = s_ve.v
                    A.wait(s_ve, mve)
                    A.activation(Pt[kc % 2][:, colstart:1024],
                                 PS[:, sb + colstart: sb + 1024], AF.Exp,
                                 bias=negc[:, kc * 16 + h: kc * 16 + h + 1])
                    exp_ac[kc] = s_ac.v
                    T.wait(s_ac, exp_ac[kc])
                    pv_from = 0 if idx == 0 else colstart
                    for off, n in _spans512(pv_from, 1024):
                        pmm = T.matmul(
                            PS[0:65, 1024 + off:1024 + off + n],
                            V_tm[:, kc * VW + h * 65: kc * VW + (h + 1) * 65],
                            Pt[kc % 2][:, off:off + n],
                            start=(idx == 0), stop=(kc == 0))
                    pmm.inc(s_mm)
                    pv_mm[kc] = s_mm.v
                V.wait(s_mm, pv_mm[0])
                V.reciprocal(rrow[:], PS[64:65, 1024:2048])
                A.wait(s_ve)
                A.copy(rrowb[:], rrow[:])
                eac = s_ac.v
                T.wait(s_ac, eac)
                for off in range(0, 1024, 512):
                    emm = T.matmul(PS[0:64, 2048 + off:2048 + off + 512],
                                   onesrb[0:1, 0:64], rrowb[0:1, off:off + 512],
                                   start=True, stop=True)
                emm.inc(s_mm)
                ev = s_mm.v
                A.wait(s_mm, ev)
                ep = ep_cnt[0]
                ep_cnt[0] += 1
                A.wait(s_ve)
                A.copy(o_t[ep % 2][0:64, :], PS[0:64, 1024:2048])
                V.wait(s_ac)
                if ep >= 2:
                    V.wait(s_stp[ep % 2], 16 * (ep // 2))
                V.tensor_mul(o_bf[ep % 2][0:64, :], o_t[ep % 2][0:64, :],
                             PS[0:64, 2048:3072])
                obv = s_ve.v
                Y.wait(s_ve, obv)
                Y.dma_start(
                    out=o_out_r[hc][hrow:hrow + 64, qh * 1024:(qh + 1) * 1024],
                    in_=o_bf[ep % 2][0:64, :]).inc(s_stp[ep % 2], 16)
                T.wait(s_ve, obv)
                T.wait(s_ac)

        Y.wait(s_stp[0], s_stp[0].v)
        Y.wait(s_stp[1], s_stp[1].v)
        p.run()
    return nc



def _emit_lnstats_chunks(p, E, PS, acc, nch, sq, onesc, onescb, s_mm, s_ac,
                         wait_ve=None):
    onesc = onescb
    """mean/sq sums over `nch` feature chunks of resident acc [128, nch*1024]
    into PS[0:1,0:1024] / PS[0:1,1024:2048]. acc may be f32r or bf16."""
    Y, G, A, V, T = E
    mm0 = s_mm.v
    for kc in range(nch):
        if wait_ve is not None:
            A.wait(p.sems_ve, wait_ve) if False else None
        if kc >= 1:
            A.wait(s_mm, mm0 + kc)
        A.activation(sq[0][:], acc[:, kc * 1024:kc * 1024 + 512], AF.Square)
        A.activation(sq[1][:], acc[:, kc * 1024 + 512:(kc + 1) * 1024], AF.Square)
        acd = s_ac.v
        T.wait(s_ac, acd)
        fi, la = (kc == 0), (kc == nch - 1)
        T.matmul(PS[0:1, 0:512], onesc[:],
                 acc[:, kc * 1024:kc * 1024 + 512], start=fi, stop=la)
        T.matmul(PS[0:1, 512:1024], onesc[:],
                 acc[:, kc * 1024 + 512:(kc + 1) * 1024], start=fi, stop=la)
        T.matmul(PS[0:1, 1024:1536], onescb[:], sq[0][:], start=fi, stop=la)
        T.matmul(PS[0:1, 1536:2048], onescb[:], sq[1][:], start=fi, stop=la).inc(s_mm)


def emit_ln_factors_dim(E, PS, st, W, onesr, s_mm, s_ve, s_ac, dim):
    Y, G, A, V, TE = E
    V.wait(s_mm)
    V.tensor_scalar_mul(st["m"][:, :W], PS[0:1, 0:W], 1.0 / dim)
    V.tensor_scalar(st["s"][:, :W], PS[0:1, 1024:1024 + W], 1.0 / dim, LN_EPS,
                    ALU.mult, ALU.add)
    V.tensor_mul(st["v"][:, :W], st["m"][:, :W], st["m"][:, :W])
    V.tensor_sub(st["v"][:, :W], st["s"][:, :W], st["v"][:, :W])
    A.wait(s_ve)
    A.activation(st["s"][:, :W], st["v"][:, :W], AF.Ln)
    A.activation(st["r"][:, :W], st["s"][:, :W], AF.Exp, scale=-0.5)
    V.wait(s_ac)
    V.tensor_mul(st["mr"][:, :W], st["m"][:, :W], st["r"][:, :W])
    A.wait(s_ve)
    A.mul(st["nmr"][:, :W], st["mr"][:, :W], -1.0)
    TE.wait(s_ac)
    for off in range(0, W, 512):
        n = min(512, W - off)
        TE.matmul(PS[:, 2048 + off:2048 + off + n], onesr[:],
                  st["r"][0:1, off:off + n], start=True, stop=True)
        op = TE.matmul(PS[:, 3072 + off:3072 + off + n], onesr[:],
                       st["nmr"][0:1, off:off + n], start=True, stop=True)
    op.inc(s_mm)


def build_attn1kv():
    nc = bass.Bass("TRN2", target_bir_lowering=False)
    xnin = nc.dram_tensor("xnin", [D, TOK], BF16, kind="ExternalInput").ap()
    wqd = nc.dram_tensor("wqd", [D, Q_RANK], F32, kind="ExternalInput").ap()
    qg = nc.dram_tensor("qg", [128, 12], F32, kind="ExternalInput").ap()
    qb = nc.dram_tensor("qb", [128, 12], F32, kind="ExternalInput").ap()
    wqu = nc.dram_tensor("wqu", [Q_RANK, D], F32, kind="ExternalInput").ap()
    wqr = nc.dram_tensor("wqr", [Q_RANK, 1024], F32, kind="ExternalInput").ap()
    wqrr = nc.dram_tensor("wqrr", [Q_RANK, 1024], F32, kind="ExternalInput").ap()
    wkvd = nc.dram_tensor("wkvd", [D, KV_RANK], F32, kind="ExternalInput").ap()
    kg = nc.dram_tensor("kg", [128, 4], F32, kind="ExternalInput").ap()
    kb = nc.dram_tensor("kb", [128, 4], F32, kind="ExternalInput").ap()
    wku = nc.dram_tensor("wku", [KV_RANK, D], F32, kind="ExternalInput").ap()
    wkr = nc.dram_tensor("wkr", [KV_RANK, 1024], F32, kind="ExternalInput").ap()
    wkrr = nc.dram_tensor("wkrr", [KV_RANK, 1024], F32, kind="ExternalInput").ap()
    wvu = nc.dram_tensor("wvu", [KV_RANK, D], F32, kind="ExternalInput").ap()
    cosq = nc.dram_tensor("cosq", [128, TOK], F32, kind="ExternalInput").ap()
    sinq = nc.dram_tensor("sinq", [128, TOK], F32, kind="ExternalInput").ap()
    cosk = nc.dram_tensor("cosk", [128, TOK], F32, kind="ExternalInput").ap()
    sink = nc.dram_tensor("sink", [128, TOK], F32, kind="ExternalInput").ap()
    onesc_in = nc.dram_tensor("onesc", [128, 1], F32, kind="ExternalInput").ap()
    onesr_in = nc.dram_tensor("onesr", [1, 128], F32, kind="ExternalInput").ap()

    q_out = nc.dram_tensor("q_out", [3072, TOK], BF16, kind="ExternalOutput").ap()
    k_out = nc.dram_tensor("k_out", [D, TOK], BF16, kind="ExternalOutput").ap()
    kr_out = nc.dram_tensor("kr_out", [1024, TOK], BF16, kind="ExternalOutput").ap()
    v_out = nc.dram_tensor("v_out", [TOK, D], BF16, kind="ExternalOutput").ap()

    xnin_r = xnin.rearrange("(kc p) t -> kc p t", p=128)
    q_out_r = q_out.rearrange("(mc p) t -> mc p t", p=128)
    k_out_r = k_out.rearrange("(mc p) t -> mc p t", p=128)
    kr_out_r = kr_out.rearrange("(mc p) t -> mc p t", p=128)
    v_out_r = v_out.rearrange("(tc p) d -> tc p d", p=128)

    SC = float(1.0 / np.sqrt(HD1 + ROPE))

    with ExitStack() as ctx:
        p = Prog(nc, ctx)
        s_ac = p.sem("s_ac")
        s_ve = p.sem("s_ve")
        E = p.engs(s_ac, s_ve)
        Y, G, A, V, T = E

        xn = p.sb("xn", [128, KC * 1024], BF16)
        tq = p.sb("tq", [128, 12 * 1024], BF16)
        cq = p.sb("cq", [128, 12 * 1024], BF16)
        tkv = p.sb("tkv", [128, 4 * 1024], BF16)
        ckv = p.sb("ckv", [128, 4 * 1024], BF16)
        wst = [p.sb(f"wst{i}", [128, KC, 128], BF16) for i in range(2)]
        qg_s = p.sb("qg_s", [128, 12], F32)
        qb_s = p.sb("qb_s", [128, 12], F32)
        kg_s = p.sb("kg_s", [128, 4], F32)
        kb_s = p.sb("kb_s", [128, 4], F32)
        wvu_sb = p.sb("wvu_sb", [128, 4, D], BF16)
        cosq_s = p.sb("cosq_s", [128, 1024], F32)
        sinq_s = p.sb("sinq_s", [128, 1024], F32)
        cosk_s = p.sb("cosk_s", [128, 1024], F32)
        sink_s = p.sb("sink_s", [128, 1024], F32)
        ev_t = [p.sb(f"ev{i}", [128, 1024], BF16) for i in range(2)]
        rp_t = [p.sb(f"rp{i}", [128, 1024], F32) for i in range(2)]
        rp2_t = [p.sb(f"rp2_{i}", [128, 1024], F32) for i in range(2)]
        sq = [p.sb(f"sq{i}", [128, 512], BF16) for i in range(2)]
        onescb = p.sb("onescb", [128, 1], BF16)
        onesc = p.sb("onesc_s", [128, 1], F32R)
        onesr = p.sb("onesr_s", [1, 128], F32R)
        st1 = ln_stat_tiles(p, "ln1")
        PS = p.ps("PS", [128, 4096], F32)

        s_n = [p.sem("s_n0"), p.sem("s_n1")]
        s_w = [p.sem("s_w0"), p.sem("s_w1")]
        s_cn = p.sem("s_cn")
        s_mm = p.sem("s_mm")
        s_stp = [p.sem("s_st0"), p.sem("s_st1")]
        stc = [0]

        for t_, src in ((onescb, onesc_in), (qg_s, qg), (qb_s, qb),
                        (kg_s, kg), (kb_s, kb), (cosq_s, cosq), (sinq_s, sinq),
                        (cosk_s, cosk), (sink_s, sink)):
            G.dma_start(out=t_[:], in_=src[:]).inc(s_cn, 16)
        G.dma_start(out=onesc[:], in_=onesc_in.bitcast(F32R)).inc(s_cn, 16)
        G.dma_start(out=onesr[:], in_=onesr_in.bitcast(F32R)).inc(s_cn, 16)
        G.dma_start(out=wvu_sb[:, :, :],
                    in_=wvu.rearrange("(kc p) d -> p kc d", p=128)).inc(s_cn, 16)
        cn_all = s_cn.v
        for kc in range(KC):
            Y.dma_start(out=xn[:, kc * 1024:(kc + 1) * 1024],
                        in_=xnin_r[kc]).inc(s_n[kc % 2], 16)

        uctr = [0]
        unit_mm = []

        def proj(wsrc, nmc, kin, src_sb, evac, extra_wait=None):
            for mc in range(nmc):
                u = uctr[0]
                uctr[0] += 1
                if u >= 2:
                    G.wait(s_mm, unit_mm[u - 2])
                G.dma_start(
                    out=wst[u % 2][:, 0:kin, :],
                    in_=wsrc[:, mc * 128:(mc + 1) * 128]
                    .rearrange("(kc p) mm -> p kc mm", p=128)).inc(s_w[u % 2], 16)
                wn = s_w[u % 2].v
                T.wait(s_w[u % 2], wn)
                if extra_wait is not None and mc == 0:
                    T.wait(s_ve, extra_wait[0])
                    T.wait(s_ac, extra_wait[1])
                for nt in range(2):
                    step = u * 2 + nt
                    bank = step % 2
                    if u > 0 or nt > 0:
                        T.wait(s_ac)
                        T.wait(s_ve)
                    for kc in range(kin):
                        mmop = T.matmul(
                            PS[:, bank * 512:(bank + 1) * 512],
                            wst[u % 2][:, kc, :],
                            src_sb[:, kc * 1024 + nt * 512: kc * 1024 + (nt + 1) * 512],
                            start=(kc == 0), stop=(kc == kin - 1))
                    mmop.inc(s_mm)
                    mmv = s_mm.v
                    evac(mc, nt, bank, mmv)
                unit_mm.append(s_mm.v)

        T.wait(s_cn, cn_all)
        T.wait(s_n[0], s_n[0].v)
        T.wait(s_n[1], s_n[1].v)

        def ev_tq(mc, nt, bank, mmv):
            A.wait(s_mm, mmv)
            A.copy(tq[:, mc * 1024 + nt * 512: mc * 1024 + (nt + 1) * 512],
                   PS[:, bank * 512:(bank + 1) * 512])
        proj(wqd, 12, KC, xn, ev_tq)
        def ev_tkv(mc, nt, bank, mmv):
            A.wait(s_mm, mmv)
            A.copy(tkv[:, mc * 1024 + nt * 512: mc * 1024 + (nt + 1) * 512],
                   PS[:, bank * 512:(bank + 1) * 512])
        proj(wkvd, 4, KC, xn, ev_tkv)
        tq_ac = s_ac.v

        A.wait(s_ac, tq_ac)
        _emit_lnstats_chunks(p, E, PS, tq, 12, sq, onesc, onescb, s_mm, s_ac)
        emit_ln_factors_dim(E, PS, st1, 1024, onesr, s_mm, s_ve, s_ac, Q_RANK)
        repq = s_mm.v
        for kc in range(12):
            V.wait(s_mm, repq)
            V.tensor_mul(rp_t[0][:, 0:512], tq[:, kc * 1024:kc * 1024 + 512],
                         PS[:, 2048:2560])
            V.tensor_mul(rp_t[0][:, 512:1024],
                         tq[:, kc * 1024 + 512:(kc + 1) * 1024], PS[:, 2560:3072])
            V.tensor_add(rp_t[0][:, 0:512], rp_t[0][:, 0:512], PS[:, 3072:3584])
            V.tensor_add(rp_t[0][:, 512:1024], rp_t[0][:, 512:1024], PS[:, 3584:4096])
            V.tensor_scalar(cq[:, kc * 1024:(kc + 1) * 1024], rp_t[0][:],
                            qg_s[:, kc:kc + 1], qb_s[:, kc:kc + 1],
                            ALU.mult, ALU.add)
        cq_ve = s_ve.v

        V.wait(s_ve, cq_ve)
        _emit_lnstats_chunks(p, E, PS, tkv, 4, sq, onesc, onescb, s_mm, s_ac)
        emit_ln_factors_dim(E, PS, st1, 1024, onesr, s_mm, s_ve, s_ac, KV_RANK)
        repk = s_mm.v
        for kc in range(4):
            V.wait(s_mm, repk)
            V.tensor_mul(rp_t[0][:, 0:512], tkv[:, kc * 1024:kc * 1024 + 512],
                         PS[:, 2048:2560])
            V.tensor_mul(rp_t[0][:, 512:1024],
                         tkv[:, kc * 1024 + 512:(kc + 1) * 1024], PS[:, 2560:3072])
            V.tensor_add(rp_t[0][:, 0:512], rp_t[0][:, 0:512], PS[:, 3072:3584])
            V.tensor_add(rp_t[0][:, 512:1024], rp_t[0][:, 512:1024], PS[:, 3584:4096])
            V.tensor_scalar(ckv[:, kc * 1024:(kc + 1) * 1024], rp_t[0][:],
                            kg_s[:, kc:kc + 1], kb_s[:, kc:kc + 1],
                            ALU.mult, ALU.add)
        ckv_ve = s_ve.v

        def mk_store_evac(out_r, base_mc, scale):
            def ev(mc, nt, bank, mmv):
                A.wait(s_mm, mmv)
                ep = stc[0]
                stc[0] += 1
                if ep >= 2:
                    A.wait(s_stp[ep % 2], 16 * (ep // 2))
                buf = ep % 2
                if scale is None:
                    A.copy(ev_t[buf][:, nt * 512:(nt + 1) * 512],
                           PS[:, bank * 512:(bank + 1) * 512])
                else:
                    A.mul(ev_t[buf][:, nt * 512:(nt + 1) * 512],
                          PS[:, bank * 512:(bank + 1) * 512], scale)
                aw = s_ac.v
                Y.wait(s_ac, aw)
                Y.dma_start(out=out_r[base_mc + mc][:, nt * 512:(nt + 1) * 512],
                            in_=ev_t[buf][:, nt * 512:(nt + 1) * 512]
                            ).inc(s_stp[ep % 2], 16)
            return ev
        T.wait(s_ve, ckv_ve)
        proj(wqu, KC, 12, cq, mk_store_evac(q_out_r, 0, SC))

        def rope_pass(wr, wrr, cos_s, sin_s, out_r, base_mc, nmc, kin, src_sb):
            for mc in range(nmc):
                for (wsrc, dst) in ((wr, rp_t), (wrr, rp2_t)):
                    u = uctr[0]
                    uctr[0] += 1
                    if u >= 2:
                        G.wait(s_mm, unit_mm[u - 2])
                    G.dma_start(
                        out=wst[u % 2][:, 0:kin, :],
                        in_=wsrc[:, mc * 128:(mc + 1) * 128]
                        .rearrange("(kc p) mm -> p kc mm", p=128)).inc(s_w[u % 2], 16)
                    wn = s_w[u % 2].v
                    T.wait(s_w[u % 2], wn)
                    for nt in range(2):
                        bank = (uctr[0] * 2 + nt) % 2
                        T.wait(s_ac)
                        T.wait(s_ve)
                        for kc in range(kin):
                            mmop = T.matmul(
                                PS[:, bank * 512:(bank + 1) * 512],
                                wst[u % 2][:, kc, :],
                                src_sb[:, kc * 1024 + nt * 512: kc * 1024 + (nt + 1) * 512],
                                start=(kc == 0), stop=(kc == kin - 1))
                        mmop.inc(s_mm)
                        mmv = s_mm.v
                        A.wait(s_mm, mmv)
                        A.copy(dst[0][:, nt * 512:(nt + 1) * 512],
                               PS[:, bank * 512:(bank + 1) * 512])
                    unit_mm.append(s_mm.v)
                V.wait(s_ac)
                ep = stc[0]
                stc[0] += 1
                if ep >= 2:
                    V.wait(s_stp[ep % 2], 16 * (ep // 2))
                buf = ep % 2
                V.tensor_mul(rp_t[1][:], rp_t[0][:], cos_s[:])
                V.tensor_mul(rp2_t[1][:], rp2_t[0][:], sin_s[:])
                V.tensor_add(ev_t[buf][:], rp_t[1][:], rp2_t[1][:])
                vw = s_ve.v
                Y.wait(s_ve, vw)
                Y.dma_start(out=out_r[base_mc + mc][:],
                            in_=ev_t[buf][:]).inc(s_stp[ep % 2], 16)
        rope_pass(wqr, wqrr, cosq_s, sinq_s, q_out_r, 16, 8, 12, cq)

        proj(wku, KC, 4, ckv, mk_store_evac(k_out_r, 0, None))
        rope_pass(wkr, wkrr, cosk_s, sink_s, kr_out_r, 0, 8, 4, ckv)
        for tc in range(8):
            for nd in range(4):
                bank = (tc * 4 + nd) % 2
                T.wait(s_ac)
                for kc in range(4):
                    mmop = T.matmul(
                        PS[:, bank * 512:(bank + 1) * 512],
                        ckv[:, kc * 1024 + tc * 128: kc * 1024 + (tc + 1) * 128],
                        wvu_sb[:, kc, nd * 512:(nd + 1) * 512],
                        start=(kc == 0), stop=(kc == 3))
                mmop.inc(s_mm)
                mmv = s_mm.v
                ep = stc[0]
                stc[0] += 1
                A.wait(s_mm, mmv)
                if ep >= 2:
                    A.wait(s_stp[ep % 2], 16 * (ep // 2))
                buf = ep % 2
                A.copy(ev_t[buf][:, 0:512], PS[:, bank * 512:(bank + 1) * 512])
                aw = s_ac.v
                Y.wait(s_ac, aw)
                Y.dma_start(out=v_out_r[tc][:, nd * 512:(nd + 1) * 512],
                            in_=ev_t[buf][:, 0:512]).inc(s_stp[ep % 2], 16)

        Y.wait(s_stp[0], s_stp[0].v)
        Y.wait(s_stp[1], s_stp[1].v)
        p.run()
    return nc



def build_attn1q():
    nc = bass.Bass("TRN2", target_bir_lowering=False)
    qc_in = nc.dram_tensor("qc_in", [1024, S], BF16, kind="ExternalInput").ap()
    qr_in = nc.dram_tensor("qr_in", [512, S], BF16, kind="ExternalInput").ap()
    kc_in = nc.dram_tensor("kc_in", [1024, S], BF16, kind="ExternalInput").ap()
    kr_in = nc.dram_tensor("kr_in", [512, S], BF16, kind="ExternalInput").ap()
    v_in = nc.dram_tensor("v_in", [S, 1024], BF16, kind="ExternalInput").ap()
    onesc_in = nc.dram_tensor("onesc", [128, 1], F32, kind="ExternalInput").ap()
    onesr_in = nc.dram_tensor("onesr", [1, 128], F32, kind="ExternalInput").ap()

    o_out = nc.dram_tensor("o_out", [1024, S], BF16, kind="ExternalOutput").ap()
    o_out_r = o_out.rearrange("(hc p) t -> hc p t", p=128)

    with ExitStack() as ctx:
        p = Prog(nc, ctx)
        s_ac = p.sem("s_ac")
        s_ve = p.sem("s_ve")
        E = p.engs(s_ac, s_ve)
        Y, G, A, V, T = E

        qc = p.sb("qc", [128, 8 * S], BF16)
        qr = p.sb("qr", [128, 4 * S], BF16)
        kc = p.sb("kc", [128, 8 * S], BF16)
        kr = p.sb("kr", [128, 4 * S], BF16)
        vt = p.sb("vt", [128, 16 * 1024], BF16)
        Pt = [p.sb(f"Pt{i}", [128, 1024], BF16) for i in range(2)]
        o_t = [p.sb(f"o_t{i}", [128, 1024], F32) for i in range(2)]
        o_bf = [p.sb(f"o_bf{i}", [128, 1024], BF16) for i in range(2)]
        rrow = p.sb("rrow", [1, 1024], F32)
        rrowb = p.sb("rrowb", [1, 1024], BF16)
        onescb = p.sb("onescb", [128, 1], BF16)
        onesrb = p.sb("onesrb", [1, 128], BF16)
        PS = p.ps("PS", [128, 4096], F32)

        s_n = [p.sem("s_n0"), p.sem("s_n1")]
        s_cn = p.sem("s_cn")
        s_mm = p.sem("s_mm")
        s_stp = [p.sem("s_st0"), p.sem("s_st1")]
        ep_cnt = [0]

        G.dma_start(out=onescb[:], in_=onesc_in[:]).inc(s_cn, 16)
        G.dma_start(out=onesrb[:], in_=onesr_in[:]).inc(s_cn, 16)
        loads = [(qc, qc_in, 8), (qr, qr_in, 4), (kc, kc_in, 8), (kr, kr_in, 4)]
        li = 0
        for dst, src, nch in loads:
            src_r = src.rearrange("(mc p) t -> mc p t", p=128)
            for mc in range(nch):
                Y.dma_start(out=dst[:, mc * S:(mc + 1) * S],
                            in_=src_r[mc]).inc(s_n[li % 2], 16)
                li += 1
        v_r = v_in.rearrange("(tc p) d -> tc p d", p=128)
        for tc in range(16):
            Y.dma_start(out=vt[:, tc * 1024:(tc + 1) * 1024],
                        in_=v_r[tc]).inc(s_n[li % 2], 16)
            li += 1

        T.wait(s_cn, s_cn.v)
        T.wait(s_n[0], s_n[0].v)
        T.wait(s_n[1], s_n[1].v)

        for h in range(8):
            hp, hrow = h // 2, (h % 2) * 64
            for qh in range(2):
                exp_ac = {}
                pv_mm = {}
                for idx in range(16):
                    kci = idx
                    sb = 0 if kci % 2 == 0 else 3072
                    if idx >= 2:
                        T.wait(s_ac, exp_ac[kci - 2])
                    elif h + qh > 0:
                        T.wait(s_ac)
                    for off in range(0, 1024, 512):
                        T.matmul(PS[:, sb + off: sb + off + 512],
                                 kc[:, h * S + kci * 128: h * S + (kci + 1) * 128],
                                 qc[:, h * S + qh * 1024 + off: h * S + qh * 1024 + off + 512],
                                 start=True, stop=False)
                        smm = T.matmul(
                            PS[:, sb + off: sb + off + 512],
                            kr[hrow:hrow + 64, hp * S + kci * 128: hp * S + (kci + 1) * 128],
                            qr[hrow:hrow + 64, hp * S + qh * 1024 + off: hp * S + qh * 1024 + off + 512],
                            start=False, stop=True)
                    smm.inc(s_mm)
                    sv = s_mm.v
                    V.wait(s_mm, sv)
                    V.tensor_scalar_min(PS[:, sb: sb + 1024], PS[:, sb: sb + 1024], 60.0)
                    mve = s_ve.v
                    A.wait(s_ve, mve)
                    A.activation(Pt[kci % 2][:], PS[:, sb: sb + 1024], AF.Exp)
                    exp_ac[kci] = s_ac.v
                    T.wait(s_ac, exp_ac[kci])
                    for off in range(0, 1024, 512):
                        T.matmul(PS[:, 1024 + off:1024 + off + 512],
                                 vt[:, kci * 1024 + h * 128: kci * 1024 + (h + 1) * 128],
                                 Pt[kci % 2][:, off:off + 512],
                                 start=(kci == 0), stop=(kci == 15))
                        dmm = T.matmul(PS[0:1, 2048 + off:2048 + off + 512],
                                       onescb[:], Pt[kci % 2][:, off:off + 512],
                                       start=(kci == 0), stop=(kci == 15))
                    dmm.inc(s_mm)
                    pv_mm[kci] = s_mm.v
                V.wait(s_mm, pv_mm[15])
                V.reciprocal(rrow[:], PS[0:1, 2048:3072])
                A.wait(s_ve)
                A.copy(rrowb[:], rrow[:])
                eac = s_ac.v
                T.wait(s_ac, eac)
                for off in range(0, 1024, 512):
                    emm = T.matmul(PS[:, 2048 + off:2048 + off + 512],
                                   onesrb[:], rrowb[0:1, off:off + 512],
                                   start=True, stop=True)
                emm.inc(s_mm)
                ev = s_mm.v
                ep = ep_cnt[0]
                ep_cnt[0] += 1
                A.wait(s_mm, ev)
                A.copy(o_t[ep % 2][:], PS[:, 1024:2048])
                V.wait(s_ac)
                if ep >= 2:
                    V.wait(s_stp[ep % 2], 16 * (ep // 2))
                V.tensor_mul(o_bf[ep % 2][:], o_t[ep % 2][:], PS[:, 2048:3072])
                obv = s_ve.v
                Y.wait(s_ve, obv)
                Y.dma_start(out=o_out_r[h][:, qh * 1024:(qh + 1) * 1024],
                            in_=o_bf[ep % 2][:]).inc(s_stp[ep % 2], 16)
                T.wait(s_ve, obv)

        Y.wait(s_stp[0], s_stp[0].v)
        Y.wait(s_stp[1], s_stp[1].v)
        p.run()
    return nc



def _c(a):
    return np.ascontiguousarray(a)


def prep_ffn0_weights(w_gate, w_up, w_down):
    wg = np.stack([_c(w_gate[:, m::4].reshape(D, NG, 128).transpose(1, 0, 2))
                   for m in range(4)])
    wu = np.stack([_c(w_up[:, m::4].reshape(D, NG, 128).transpose(1, 0, 2))
                   for m in range(4)])
    wd = np.stack([_c(w_down[m::4, :]) for m in range(4)])
    return _c(wg), _c(wu), _c(wd)


ONESC = np.ones((128, 1), np.float32)
ONESR = np.ones((1, 128), np.float32)
IDEN = np.eye(128, dtype=np.float32)
TRIU = np.triu(np.ones((128, 128), np.float32))
MASKD = np.where(TRIU > 0, 0.0, -1e30).astype(np.float32)
ONES2D = np.ones((128, 128), np.float32)


def rope_tables(pos, scale):
    inv = (1.0 / 10000.0 ** (np.arange(0, ROPE, 2, dtype=np.float32) / ROPE))
    freqs = pos[:, None].astype(np.float32) * inv[None, :]
    emb = np.concatenate([freqs, freqs], axis=1)
    cos_fm = np.cos(emb).T.astype(np.float32) * scale
    sgn = np.concatenate([-np.ones(32, np.float32), np.ones(32, np.float32)])
    sin_fm = np.sin(emb).T.astype(np.float32) * sgn[:, None] * scale
    cos2 = _c(np.concatenate([cos_fm, cos_fm], axis=0))
    sin2 = _c(np.concatenate([sin_fm, sin_fm], axis=0))
    return cos2, sin2


def rot_perm(w):
    n = w.shape[1] // 64
    return _c(w.reshape(w.shape[0], n, 2, 32)[:, :, ::-1, :].reshape(w.shape))


def colsplit(v, nch):
    return _c(v.reshape(nch, 128).T)


_DBG = {}


def _dbg_cmp(tag, got_fn):
    if not _DBG:
        return
    ref = _DBG.get(tag)
    if ref is None:
        return
    g = np.asarray(got_fn(), np.float64)
    r = np.asarray(ref, np.float64)
    e = float(np.sqrt(np.mean((g - r) ** 2) / (np.mean(r ** 2) + 1e-30)))
    print(f"  [dbg] {tag}: rel={e:.4e}", flush=True)


def kernel(**inp):
    import concourse.bass_utils as bu
    x = np.asarray(inp["L0_attn_hidden_states"])

    wg, wu, wdn = prep_ffn0_weights(np.asarray(inp["L0_ffn_w_gate"]),
                                    np.asarray(inp["L0_ffn_w_up"]),
                                    np.asarray(inp["L0_ffn_w_down"]))
    Wq, Wk, Wv = (np.asarray(inp["L0_attn_Wq"]), np.asarray(inp["L0_attn_Wk"]),
                  np.asarray(inp["L0_attn_Wv"]))
    Wf, bfv, Wo = (np.asarray(inp["L0_attn_Wf"]), np.asarray(inp["L0_attn_bf"]),
                   np.asarray(inp["L0_attn_Wo"]))
    w1 = np.asarray(inp["L1_ffn_weight"])
    scl = colsplit(np.asarray(inp["L1_ffn_scale"]), KC)

    cores = list(range(NCORES))

    def run(nc, maps):
        return run_bass_kernel_spmd(nc, maps, core_ids=cores).results

    consts = {"onesc": ONESC, "onesr": ONESR}

    nc_ffn0 = build_launch1()
    maps = []
    for c in cores:
        b, hh = c // 2, c % 2
        maps.append({"xin": _c(x[b, hh * TOK:(hh + 1) * TOK, :].T),
                     "wg": wg, "wu": wu, "wd": wdn, **consts})
    r1 = run(nc_ffn0, maps)
    _dbg_cmp("x1", lambda: np.stack([
        np.concatenate([r1[2 * b]["xout"], r1[2 * b + 1]["xout"]], axis=1).T
        for b in range(B)]))

    nc_a0 = build_attn0()
    xnf = [np.concatenate([r1[2 * b]["xnout"], r1[2 * b + 1]["xnout"]], axis=1)
           for b in range(B)]
    maps = []
    for c in cores:
        b, hh = c // 2, c % 2
        sl = slice(hh * 1024, (hh + 1) * 1024)
        wqT = Wq[sl, :].T
        wkT = Wk[sl, :].T
        maps.append({
            "xnin": _c(xnf[b]),
            "wq": _c(wqT.reshape(D, 8, 128).transpose(1, 0, 2)),
            "wk": _c(wkT.reshape(D, 8, 128).transpose(1, 0, 2)),
            "wv": _c(Wv[sl, :].T),
            "wf": _c(Wf[hh * 16:(hh + 1) * 16, :].T),
            "bfb": _c(bfv[hh * 16:(hh + 1) * 16][None, :]),
            "ones2d": ONES2D, "triu": TRIU, "onesr": ONESR,
            "iden": IDEN, "maskd": MASKD})
    r2 = run(nc_a0, maps)
    _dbg_cmp("o0", lambda: np.stack([
        np.concatenate([r2[2 * b]["o_out"], r2[2 * b + 1]["o_out"]],
                       axis=0).astype(np.float32).T for b in range(B)]))

    nc_wout = build_wout2()
    of = [np.concatenate([r2[2 * b]["o_out"], r2[2 * b + 1]["o_out"]], axis=0)
          for b in range(B)]
    woT = _c(Wo.T)
    maps = []
    for c in cores:
        b, hh = c // 2, c % 2
        maps.append({"xin": r1[c]["xout"],
                     "oin": _c(of[b][:, hh * TOK:(hh + 1) * TOK]),
                     "wo": woT, **consts})
    r3a = run(nc_wout, maps)
    _dbg_cmp("x2", lambda: np.stack([
        np.concatenate([r3a[2 * b]["xout"], r3a[2 * b + 1]["xout"]], axis=1).T
        for b in range(B)]))

    maps = []
    for c in cores:
        maps.append({"xin": r3a[c]["xout"], "wg": wg, "wu": wu, "wd": wdn,
                     **consts})
    r3b = run(nc_ffn0, maps)
    _dbg_cmp("x3", lambda: np.stack([
        np.concatenate([r3b[2 * b]["xout"], r3b[2 * b + 1]["xout"]], axis=1).T
        for b in range(B)]))

    nc_ffn1a = build_ffn1(True)
    maps = []
    for c in cores:
        maps.append({"xin": r3b[c]["xout"], "xnin": r3b[c]["xnout"],
                     "w1": w1, "scl": scl, **consts})
    r3c = run(nc_ffn1a, maps)
    _dbg_cmp("x4", lambda: np.stack([
        np.concatenate([r3c[2 * b]["xout"], r3c[2 * b + 1]["xout"]], axis=1).T
        for b in range(B)]))

    nc_kv = build_attn1kv()
    SC = float(1.0 / np.sqrt(HD1 + ROPE))
    wqd = _c(np.asarray(inp["L1_attn_Wq_down"]))
    wqu = _c(np.asarray(inp["L1_attn_Wq_up"]))
    wqr = _c(np.asarray(inp["L1_attn_Wq_rope"]))
    wqrr = rot_perm(wqr)
    wkvd = _c(np.asarray(inp["L1_attn_Wkv_down"]))
    wku = _c(np.asarray(inp["L1_attn_Wk_up"]))
    wkr = _c(np.asarray(inp["L1_attn_Wk_rope"]))
    wkrr = rot_perm(wkr)
    wvu = _c(np.asarray(inp["L1_attn_Wv_up"]))
    qg = colsplit(np.asarray(inp["L1_attn_q_gamma"]), 12)
    qb = colsplit(np.asarray(inp["L1_attn_q_beta"]), 12)
    kg = colsplit(np.asarray(inp["L1_attn_kv_gamma"]), 4)
    kb = colsplit(np.asarray(inp["L1_attn_kv_beta"]), 4)
    maps = []
    for c in cores:
        b, hh = c // 2, c % 2
        pos = np.arange(hh * 1024, (hh + 1) * 1024)
        cq, sq_ = rope_tables(pos, SC)
        ck, sk = rope_tables(pos, 1.0)
        maps.append({"xnin": r3c[c]["xnout"], "wqd": wqd, "qg": qg, "qb": qb,
                     "wqu": wqu, "wqr": wqr, "wqrr": wqrr,
                     "wkvd": wkvd, "kg": kg, "kb": kb, "wku": wku,
                     "wkr": wkr, "wkrr": wkrr, "wvu": wvu,
                     "cosq": cq, "sinq": sq_, "cosk": ck, "sink": sk,
                     **consts})
    r4a = run(nc_kv, maps)

    nc_a1 = build_attn1q()
    qf = [np.concatenate([r4a[2 * b]["q_out"], r4a[2 * b + 1]["q_out"]], axis=1)
          for b in range(B)]
    kf = [np.concatenate([r4a[2 * b]["k_out"], r4a[2 * b + 1]["k_out"]], axis=1)
          for b in range(B)]
    krf = [np.concatenate([r4a[2 * b]["kr_out"], r4a[2 * b + 1]["kr_out"]], axis=1)
           for b in range(B)]
    vf = [np.concatenate([r4a[2 * b]["v_out"], r4a[2 * b + 1]["v_out"]], axis=0)
          for b in range(B)]
    maps = []
    for c in cores:
        b, hh = c // 2, c % 2
        maps.append({
            "qc_in": _c(qf[b][hh * 1024:(hh + 1) * 1024]),
            "qr_in": _c(qf[b][2048 + hh * 512: 2048 + (hh + 1) * 512]),
            "kc_in": _c(kf[b][hh * 1024:(hh + 1) * 1024]),
            "kr_in": _c(krf[b][hh * 512:(hh + 1) * 512]),
            "v_in": _c(vf[b][:, hh * 1024:(hh + 1) * 1024]),
            **consts})
    r4b = run(nc_a1, maps)
    _dbg_cmp("o1", lambda: np.stack([
        np.concatenate([r4b[2 * b]["o_out"], r4b[2 * b + 1]["o_out"]],
                       axis=0).astype(np.float32).T for b in range(B)]))

    o1f = [np.concatenate([r4b[2 * b]["o_out"], r4b[2 * b + 1]["o_out"]], axis=0)
           for b in range(B)]
    wout1 = _c(np.asarray(inp["L1_attn_Wout"]))
    maps = []
    for c in cores:
        b, hh = c // 2, c % 2
        maps.append({"xin": r3c[c]["xout"],
                     "oin": _c(o1f[b][:, hh * TOK:(hh + 1) * TOK]),
                     "wo": wout1, **consts})
    r4c = run(nc_wout, maps)
    _dbg_cmp("x5", lambda: np.stack([
        np.concatenate([r4c[2 * b]["xout"], r4c[2 * b + 1]["xout"]], axis=1).T
        for b in range(B)]))

    nc_ffn1b = build_ffn1(False)
    maps = []
    for c in cores:
        maps.append({"xin": r4c[c]["xout"], "xnin": r4c[c]["xnout"],
                     "w1": w1, "scl": scl, **consts})
    r4d = run(nc_ffn1b, maps)

    out = np.empty((B, S, D), np.float32)
    for c in cores:
        b, hh = c // 2, c % 2
        out[b, hh * TOK:(hh + 1) * TOK, :] = r4d[c]["xout"].T
    return out

